# version 03 (final)
# baseline (speedup 1.0000x reference)
import math

import numpy as np
import ml_dtypes

import concourse.bass as bass
import concourse.mybir as mybir
from concourse.bass_utils import run_bass_kernel_spmd

F32 = mybir.dt.float32
BF16 = mybir.dt.bfloat16
I32 = mybir.dt.int32
AF = mybir.ActivationFunctionType
ALU = mybir.AluOpType
AX = mybir.AxisListType
BF = ml_dtypes.bfloat16

NCORE = 8
S = 8192
D = 2048
DC = 16
TLOC = 1024
NSB = 4
EC = 68
FF = 5632
FC = 44
NEG = -1.0e30

_NF4_RAW = [-1.0, -0.6962, -0.5251, -0.3949, -0.2844, -0.1848, -0.0911,
            0.0, 0.0796, 0.1609, 0.2461, 0.3379, 0.4407, 0.5626, 0.723, 1.0]
NF4 = np.asarray(np.asarray(_NF4_RAW, np.float32).astype(BF), np.float32)
TAUS = ((NF4[:-1] + NF4[1:]) / np.float32(2)).astype(np.float32)
GAP2 = [float(g) / 2.0 for g in (NF4[1:] - NF4[:-1])]

NCH = [16 * (s + 1) for s in range(NSB)]
SBASE = [0, 16, 48, 96]
PCT = 160

SC1_N = FF * 32
SC3_N = D * 88
SC_PAD = 196608
W1_GRP = 6
W3_GRP = 2
ARENA_N = 81920

_TAPS = []


def _grp_idx(c):
    return [min(6 * c + g, 43) for g in range(W1_GRP)]


def _prep_inputs(inp):
    x = np.ascontiguousarray(np.asarray(inp["L0_attn_x"], np.float32).reshape(S, D).T)
    gamma = np.asarray(inp["L0_attn_ln_gamma"], np.float32)
    beta = np.asarray(inp["L0_attn_ln_beta"], np.float32)
    expand = np.asarray(inp["L0_attn_expand"], np.float32)
    project = np.asarray(inp["L0_attn_project"], np.float32)
    posm = np.float32(np.asarray(inp["L0_attn_pos_mult"]).reshape(()))
    w1 = np.asarray(inp["L0_ffn_weight1"])
    w2 = np.asarray(inp["L0_ffn_weight2"])
    w3 = np.asarray(inp["L0_ffn_weight3"])

    eT = np.ascontiguousarray(expand.reshape(EC, 128, DC, 128).transpose(0, 2, 3, 1))
    pT = np.ascontiguousarray(project.reshape(DC, 128, 32, 128).transpose(0, 2, 3, 1))
    w1T = np.ascontiguousarray(w1.T)
    w2T = np.ascontiguousarray(w2.T)
    w3T = np.ascontiguousarray(w3.T)
    gb = np.ascontiguousarray(gamma.reshape(DC, 128).T)
    bb = np.ascontiguousarray(beta.reshape(DC, 128).T)
    ident_f = np.eye(128, dtype=np.float32)
    taus_adj = np.nextafter(TAUS, np.float32(np.inf)).astype(np.float32)
    taus_in = np.tile((-taus_adj).reshape(1, 15), (128, 1)).astype(np.float32)
    consts = np.zeros((128, 8), np.float32)
    consts[:, 0] = 1e-5
    consts[:, 1] = -math.log(16.0)
    consts[:, 3] = 1.0
    ovec = np.zeros((1, 64), np.float32)
    for ds in range(4):
        for r in range(8):
            for jj in range(2):
                ovec[0, ds * 16 + r * 2 + jj] = -2048.0 * ds + 1024.0 * jj + r
    onesb = np.ones((128, 1), BF)
    onesf = np.ones((128, 1), np.float32)
    onesr = np.ones((1, 128), np.float32)

    p_ = np.arange(128).reshape(1, 1, 128, 1)
    j_ = np.arange(256).reshape(1, 1, 1, 256)
    r_ = np.arange(8).reshape(8, 1, 1, 1)
    jj_ = np.arange(2).reshape(1, 2, 1, 1)
    a_ = 128 * jj_ + p_

    in_maps = []
    for c in range(NCORE):
        xc = np.ascontiguousarray(x[:, c::8])
        ok = (a_ < j_) | ((a_ == j_) & (r_ <= c))
        masks = np.where(ok, np.float32(0), np.float32(NEG)).astype(BF).reshape(16, 128, 256)
        dprime = (8.0 * (np.arange(128).reshape(128, 1)
                         - np.arange(256).reshape(1, 256)) - c).astype(np.float32)
        g_idx = _grp_idx(c)
        w1nat = np.ascontiguousarray(
            np.concatenate([w1[128 * g:128 * (g + 1)] for g in g_idx], 0))
        w2nat = np.ascontiguousarray(
            np.concatenate([w2[128 * g:128 * (g + 1)] for g in g_idx], 0))
        w3nat = np.ascontiguousarray(w3[256 * c:256 * (c + 1)])
        w1Ts = np.ascontiguousarray(w1T[256 * c:256 * (c + 1)].reshape(2, 128, FF))
        w2Ts = np.ascontiguousarray(w2T[256 * c:256 * (c + 1)].reshape(2, 128, FF))
        w3Ts = np.ascontiguousarray(w3T[:, 256 * c:256 * (c + 1)].reshape(FC, 128, 256))
        rsel = np.zeros((32, 4), np.float32)
        for k in range(4):
            rsel[4 * c + k, k] = 1.0
        sel3 = np.zeros((D, 256), np.float32)
        for k in range(256):
            sel3[256 * c + k, k] = 1.0
        in_maps.append({
            "xfm": xc, "eT": eT, "pT": pT,
            "w1Ts": w1Ts, "w2Ts": w2Ts, "w3Ts": w3Ts,
            "w1nat": w1nat, "w2nat": w2nat, "w3nat": w3nat,
            "masks": masks, "dprime": dprime, "ovec": ovec,
            "gammab": gb, "betab": bb, "posm": posm.reshape(1, 1),
            "identf": ident_f, "taus": taus_in, "consts": consts,
            "onesb": onesb, "onesf": onesf, "onesr": onesr,
            "rselb": rsel.astype(BF), "rself": rsel,
            "sel3b": sel3.astype(BF), "sel3f": sel3,
        })
    return in_maps


class Toks:
    def __init__(self, prev):
        self.prev = prev if prev is not None else {}
        self.cur = {}
        self.cnt = {}

    def inc(self, sem, n=1):
        self.cnt[sem.name] = self.cnt.get(sem.name, 0) + n
        return self.cnt[sem.name]

    def val(self, sem):
        return self.cnt.get(sem.name, 0)

    def put(self, key, sem):
        self.cur[key] = self.cnt.get(sem.name, 0)

    def get(self, key):
        return self.prev.get(key, 0)


def _build(tok_prev):
    nc = bass.Bass("TRN2", target_bir_lowering=False, debug=False)
    tk = Toks(tok_prev)

    def fence(eng, sem, ins):
        ins.then_inc(sem, 1)
        tk.inc(sem)
        eng.wait_ge(sem, tk.val(sem))

    def din(name, shape, dt):
        return nc.dram_tensor(name, shape, dt, kind="ExternalInput").ap()

    xfm = din("xfm", [D, TLOC], F32)
    eT = din("eT", [EC, DC, 128, 128], F32)
    pT = din("pT", [DC, 32, 128, 128], F32)
    w1Ts = din("w1Ts", [2, 128, FF], BF16)
    w2Ts = din("w2Ts", [2, 128, FF], BF16)
    w3Ts = din("w3Ts", [FC, 128, 256], BF16)
    w1nat = din("w1nat", [W1_GRP * 128, D], BF16)
    w2nat = din("w2nat", [W1_GRP * 128, D], BF16)
    w3nat = din("w3nat", [256, FF], BF16)
    masks_i = din("masks", [16, 128, 256], BF16)
    dprime = din("dprime", [128, 256], F32)
    ovec_i = din("ovec", [1, 64], F32)
    gammab = din("gammab", [128, DC], F32)
    betab = din("betab", [128, DC], F32)
    posm = din("posm", [1, 1], F32)
    identf = din("identf", [128, 128], F32)
    taus_i = din("taus", [128, 15], F32)
    consts_i = din("consts", [128, 8], F32)
    onesb_i = din("onesb", [128, 1], BF16)
    onesf_i = din("onesf", [128, 1], F32)
    onesr_i = din("onesr", [1, 128], F32)
    rselb = din("rselb", [32, 4], BF16)
    rself = din("rself", [32, 4], F32)
    sel3b = din("sel3b", [D, 256], BF16)
    sel3f = din("sel3f", [D, 256], F32)

    out_c = nc.dram_tensor("out_c", [D, TLOC], F32, kind="ExternalOutput").ap()

    def dint(name, shape, dt):
        return nc.dram_tensor(name, shape, dt, kind="Internal").ap()

    kin = dint("kin", [2, 128, TLOC], BF16)
    kgd = dint("kgd", [NCORE, 2, 128, TLOC], BF16)
    vin = dint("vin", [8, TLOC, 256], BF16)
    vgd = dint("vgd", [NCORE, 8, TLOC, 256], BF16)
    scin = [dint(f"sc{i}in", [(W1_GRP if i < 2 else W3_GRP) * 128 * (32 if i < 2 else 88)],
                 F32) for i in range(3)]
    scg = [dint(f"sc{i}g", [NCORE * (W1_GRP if i < 2 else W3_GRP) * 128 * (32 if i < 2 else 88)],
                F32) for i in range(3)]
    deqf = [dint(f"deqf{i}", [SC_PAD], BF16) for i in range(3)]
    rcpf = [dint(f"rcpf{i}", [SC_PAD], F32) for i in range(3)]
    ownb_d = dint("ownb_d", [2, 4, FF], BF16)
    ownf_d = dint("ownf_d", [2, 4, FF], F32)
    own3b_d = dint("own3b_d", [88, 256], BF16)
    own3f_d = dint("own3f_d", [88, 256], F32)
    wdql = [dint("wdq1l", [2, FC, 128, 128], BF16), dint("wdq2l", [2, FC, 128, 128], BF16),
            dint("wdq3l", [2, FC, 128, 128], BF16)]
    wdqg = [dint("wdq1g", [NCORE, 2, FC, 128, 128], BF16),
            dint("wdq2g", [NCORE, 2, FC, 128, 128], BF16),
            dint("wdq3g", [NCORE, 2, FC, 128, 128], BF16)]
    gegsp = dint("gegsp", [16, 128, TLOC], BF16)
    attnb = dint("attnb", [16, NSB, 128, 256], BF16)

    tap_srcs = {"deqf0": deqf[0], "deqf1": deqf[1], "deqf2": deqf[2],
                "rcpf0": rcpf[0], "rcpf2": rcpf[2],
                "wdq1g": wdqg[0], "wdq2g": wdqg[1], "wdq3g": wdqg[2],
                "kgd": kgd, "vgd": vgd, "gegsp": gegsp, "attnb": attnb,
                "kin": kin, "vin": vin, "ownb_d": ownb_d, "own3b_d": own3b_d,
                "ownf_d": ownf_d, "wdq2l": wdql[1],
                "sc0g": scg[0], "sc2g": scg[2]}
    tap_outs = {}
    for t in _TAPS:
        if t in tap_srcs:
            src = tap_srcs[t]
            tap_outs[t] = nc.dram_tensor("tap_" + t, list(src.shape), src.dtype,
                                         kind="ExternalOutput").ap()
    tap_xn = (nc.dram_tensor("tap_xn", [128, DC * TLOC], BF16, kind="ExternalOutput").ap()
              if "xn" in _TAPS else None)
    tap_h2 = (nc.dram_tensor("tap_h2", [128, DC * TLOC], BF16, kind="ExternalOutput").ap()
              if "h2" in _TAPS else None)
    tap_pc = (nc.dram_tensor("tap_pc", [128, PCT * 256], BF16, kind="ExternalOutput").ap()
              if "pc" in _TAPS else None)
    tap_prj = (nc.dram_tensor("tap_prj", [128, DC * TLOC], BF16, kind="ExternalOutput").ap()
               if "prj" in _TAPS else None)
    tap_h = (nc.dram_tensor("tap_h", [128, FC * TLOC], BF16, kind="ExternalOutput").ap()
             if "h" in _TAPS else None)
    dbg = {}
    for nm, shape, dt_ in [("dbg_rowq", [1, TLOC], F32), ("dbg_spo", [128, 64], F32),
                           ("dbg_spdp", [128, 256], F32), ("dbg_tbs", [128, 1024], F32),
                           ("dbg_rl", [128, 1024], F32), ("dbg_pcc", [128, PCT * 256], BF16),
                           ("dbg_b", [128, 4 * 256], F32), ("dbg_msk", [128, 16 * 256], BF16)]:
        if nm in _TAPS:
            dbg[nm] = nc.dram_tensor("tap_" + nm, shape, dt_, kind="ExternalOutput").ap()

    sems = {}
    sem_ctxs = []
    for name in ["sLd", "sG", "sT", "sV", "sC", "sCC", "sOut"]:
        ctx = nc.semaphore(name)
        sems[name] = ctx.__enter__()
        sem_ctxs.append(ctx)
    sLd, sG, sT, sV, sC, sCC, sOut = (sems[n] for n in
                                      ["sLd", "sG", "sT", "sV", "sC", "sCC", "sOut"])

    sb_ctxs = []

    def sb(name, shape, dt):
        ctx = nc.sbuf_tensor(name, shape, dt)
        t = ctx.__enter__()
        sb_ctxs.append(ctx)
        return t

    gam_sb = sb("gam_sb", [128, DC], F32)
    bet_sb = sb("bet_sb", [128, DC], F32)
    taus_sb = sb("taus_sb", [128, 15], F32)
    cst_sb = sb("cst_sb", [128, 8], F32)
    idf_sb = sb("idf_sb", [128, 128], F32)
    ones_sb = sb("ones_sb", [128, 1], BF16)
    zeros_sb = sb("zeros_sb", [128, 512], BF16)
    onesf_sb = sb("onesf_sb", [128, 1], F32)
    onesr_sb = sb("onesr_sb", [1, 128], F32)
    dp_sb = sb("dp_sb", [128, 256], F32)
    spdp_sb = sb("spdp_sb", [128, 256], F32)
    spo_sb = sb("spo_sb", [128, 64], F32)
    spc_sb = sb("spc_sb", [128, 1], F32)
    rsb_sb = sb("rsb_sb", [32, 4], BF16)
    rsf_sb = sb("rsf_sb", [32, 4], F32)
    row64 = sb("row64", [1, 64], F32)
    rowq = sb("rowq", [1, TLOC], F32)
    rowl = sb("rowl", [1, TLOC], F32)
    colx = sb("colx", [128, 16], F32)
    rows6 = sb("rows6", [1, 8], F32)
    mbc_sb = sb("mbc_sb", [128, 4], F32)
    xn_sb = sb("xn_sb", [128, DC * TLOC], BF16)
    arena = sb("arena", [128, ARENA_N], BF16)

    A_BF = arena[:, :]
    A_F32 = arena[:].bitcast(F32)
    A_I32 = arena[:].bitcast(I32)

    def bfv(off, n):
        assert off + n <= ARENA_N, (off, n)
        return A_BF[:, off:off + n]

    def f32v(off, n):
        assert off % 2 == 0 and off + 2 * n <= ARENA_N, (off, n)
        return A_F32[:, off // 2:off // 2 + n]

    def i32v(off, n):
        assert off % 2 == 0 and off + 2 * n <= ARENA_N, (off, n)
        return A_I32[:, off // 2:off // 2 + n]

    pb_ctxs = [nc.psum_tensor(f"pb{i}", [128, 512], F32) for i in range(8)]
    pb = [c.__enter__() for c in pb_ctxs]

    d_wsl = bfv(0, FF)
    d_scl = bfv(5632, FF)
    d_dqb = bfv(11264, FF)
    d_wdq = bfv(16896, FF)
    d_rqb = f32v(22528, FF)
    d_acc = f32v(33792, FF)
    d_sgA = f32v(45056, FF)
    d_sgB = f32v(56320, FF)
    d_nat = [bfv(0, FF), bfv(5632, FF)]
    d_amax = f32v(11264, 14 * 96)
    d_sc = f32v(16384, 6 * 256)
    d_scb = bfv(19456, 6 * 256)
    d_deqt = bfv(20992, 6 * 256)
    d_rcpt = f32v(22528, 6 * 256)
    d_qv = f32v(25600, 256)
    d_qi = i32v(26112, 256)
    d_pp = bfv(26624, 256)
    d_tb = bfv(26880, 256)
    d_trL = [bfv(27136, 128), bfv(27264, 128)]
    d_trA = [f32v(27392, 128), f32v(27648, 128)]
    d_ownb = bfv(0, 2 * FF + 256)
    d_ownf = f32v(11520, 2 * FF + 256)
    d_s3b = [bfv(34560, 128), bfv(34688, 128)]
    d_s3f = [f32v(34816, 128), f32v(35072, 128)]
    d_sel3b = [bfv(35328, 256), bfv(35584, 256)]
    d_sel3f = [f32v(35840, 256), f32v(36352, 256)]
    d_L12r = f32v(45056, 2 * FF)
    d_L12d = bfv(67584, 2 * FF)
    p_h2 = bfv(0, DC * TLOC)
    p_xstg = [f32v(16384, 4096), f32v(24576, 4096)]
    p_xb = bfv(32768, 4096)
    p_x2b = bfv(36864, 4096)
    p_t1 = f32v(40960, 4096)
    p_vfm = bfv(16384, DC * TLOC)
    p_estg = [f32v(32768, 2048), f32v(36864, 2048)]
    p_ebf = [bfv(40960, 2048), bfv(43008, 2048)]
    p_lin = f32v(49152, TLOC)
    p_gel = f32v(51200, TLOC)
    p_gst = [bfv(53248, TLOC), bfv(54272, TLOC)]
    p_q = bfv(72704, 2 * TLOC)
    p_kown = bfv(57344, 2 * TLOC)
    p_vtm = bfv(0, DC * TLOC)
    p_trV = [f32v(32768, 128), f32v(33024, 128)]
    p_pc = bfv(0, PCT * 256)
    p_kgs = bfv(45056, 2 * 8192)
    p_k2s = [bfv(61440, 512), bfv(62464, 512)]
    p_q2b = bfv(63488, 2048)
    p_msk = bfv(65536, 16 * 256)
    p_tbs = f32v(69632, 4 * 256)
    p_bS = [f32v(71680, 256), f32v(72192, 256)]
    p_vpan = [bfv(45056, 64 * 256), bfv(61440, 64 * 256)]
    p_rl = f32v(77824, 4 * 256)
    p_ast = bfv(79872, 8 * 256)
    p_pstg = [f32v(0, 4096), f32v(8192, 4096)]
    p_pbf = [bfv(16384, 4096), bfv(69632, 4096)]
    p_glo = bfv(20480, DC * TLOC)
    p_att = bfv(36864, DC * TLOC)
    p_prj = bfv(53248, DC * TLOC)
    p_h = bfv(0, FC * TLOC)
    p_wst = [[bfv(69632, 2048), bfv(71680, 2048)],
             [bfv(75776, 2048), bfv(77824, 2048)]]
    p_sil = [f32v(73728, TLOC), f32v(79872, TLOC)]
    p_w3st = [bfv(69632, 22 * 128), bfv(72448, 22 * 128)]
    p_xst2 = f32v(75264, TLOC)

    d4b = [("12", 0, 0), ("12", 0, 1), ("12", 1, 0), ("12", 1, 1),
           ("3", 2, 0), ("3", 2, 1)]
    RG = [list(range(NCORE))]
    ec_order = [0, 1, 2, 3] + [x for i in range(32) for x in (4 + i, 36 + i)]

    with nc.Block() as b:
        @b.scalar
        def _(eng):
            i = 0
            for nat, n_grp, w_ in [(w1nat, W1_GRP, D), (w2nat, W1_GRP, D),
                                   (w3nat, W3_GRP, FF)]:
                for g in range(n_grp):
                    if i >= 2:
                        eng.wait_ge(sV, tk.get(("d0am", i - 2)))
                    eng.dma_start(out=d_nat[i % 2][:, 0:w_],
                                  in_=nat[128 * g:128 * (g + 1), :]).then_inc(sC, 16)
                    tk.inc(sC, 16)
                    tk.put(("d0ld", i), sC)
                    i += 1

        @b.vector
        def _(eng):
            eng.memset(zeros_sb[:, :], 0.0)
            i = 0
            for wi, (n_grp, nj, w_) in enumerate(
                    [(W1_GRP, 32, D), (W1_GRP, 32, D), (W3_GRP, 88, FF)]):
                for g in range(n_grp):
                    eng.wait_ge(sC, tk.get(("d0ld", i)))
                    fence(eng, sV, eng.reduce_max(
                        out=d_amax[:, 96 * i:96 * i + nj],
                        in_=d_nat[i % 2][:, 0:w_].rearrange("p (g k) -> p g k", k=64),
                        axis=AX.X, apply_absolute_value=True))
                    tk.put(("d0am", i), sV)
                    i += 1

        @b.sync
        def _(eng):
            for dst, src in [(gam_sb[:, :], gammab[:, :]), (bet_sb[:, :], betab[:, :]),
                             (taus_sb[:, :], taus_i[:, :]), (cst_sb[:, :], consts_i[:, :]),
                             (idf_sb[:, :], identf[:, :]), (dp_sb[:, :], dprime[:, :]),
                             (ones_sb[:, :], onesb_i[:, :]), (onesf_sb[:, :], onesf_i[:, :]),
                             (onesr_sb[:, :], onesr_i[:, :]), (rsb_sb[:, :], rselb[:, :]),
                             (rsf_sb[:, :], rself[:, :]), (row64[0:1, :], ovec_i[0:1, :])]:
                eng.dma_start(out=dst, in_=src).then_inc(sLd, 16)
                tk.inc(sLd, 16)
            i = 0
            for wi, (n_grp, nj) in enumerate([(W1_GRP, 32), (W1_GRP, 32), (W3_GRP, 88)]):
                for g in range(n_grp):
                    eng.wait_ge(sV, tk.get(("d0am", i)))
                    eng.dma_start(
                        out=scin[wi][128 * nj * g:128 * nj * (g + 1)]
                        .rearrange("(p k) -> p k", p=128),
                        in_=d_amax[:, 96 * i:96 * i + nj]).then_inc(sLd, 16)
                    tk.inc(sLd, 16)
                    i += 1
            eng.wait_ge(sLd, tk.val(sLd))
            tk.put(("d0done",), sLd)

    with nc.Block() as b:
        @b.gpsimd
        def _(eng):
            eng.wait_ge(sLd, tk.get(("d0done",)))
            for i in range(3):
                eng.collective_compute("AllGather", ALU.bypass, replica_groups=RG,
                                       ins=[scin[i]], outs=[scg[i]]).then_inc(sCC, 1)
                tk.inc(sCC)
            eng.wait_ge(sCC, tk.val(sCC))

    for wi in range(3):
        n_real = SC1_N if wi < 2 else SC3_N
        with nc.Block() as b:
            @b.sync
            def _(eng, wi=wi):
                if wi < 2:
                    eng.dma_start(out=d_sc[:, :].rearrange("p (a k) -> p a k", k=256),
                                  in_=scg[wi][0:SC_PAD]
                                  .rearrange("(a p k) -> p a k", p=128, k=256)
                                  ).then_inc(sLd, 16)
                    tk.inc(sLd, 16)
                else:
                    eng.wait_ge(sV, tk.get(("d2pad",)))
                    eng.dma_start(out=d_sc[:, 0:5 * 256].rearrange("p (a k) -> p a k", k=256),
                                  in_=scg[wi][0:5 * 32768]
                                  .rearrange("(a p k) -> p a k", p=128, k=256)
                                  ).then_inc(sLd, 16)
                    eng.dma_start(out=d_sc[0:64, 5 * 256:6 * 256],
                                  in_=scg[wi][5 * 32768:SC3_N]
                                  .rearrange("(p k) -> p k", p=64)).then_inc(sLd, 16)
                    tk.inc(sLd, 32)
                tk.put(("d2ld", wi), sLd)

            @b.vector
            def _(eng, wi=wi):
                if wi == 2:
                    fence(eng, sV, eng.memset(d_sc[:, 5 * 256:6 * 256], 1.0))
                    tk.put(("d2pad",), sV)
                eng.wait_ge(sLd, tk.get(("d2ld", wi)))
                fence(eng, sV, eng.reduce_sum(
                    out=colx[:, 0:5],
                    in_=d_sc[:, 0:5 * 256].rearrange("p (a k) -> p a k", k=256),
                    axis=AX.X))
                fence(eng, sV, eng.memset(colx[:, 5:6], 0.0))
                fence(eng, sV, eng.reduce_sum(out=colx[0:64, 5:6],
                                              in_=d_sc[0:64, 5 * 256:6 * 256], axis=AX.X))
                fence(eng, sV, eng.reduce_sum(out=colx[:, 6:7],
                                              in_=colx[:, 0:6].unsqueeze(1), axis=AX.X))
                tk.put(("d2sums", wi), sV)

            @b.tensor
            def _(eng, wi=wi):
                eng.wait_ge(sV, tk.get(("d2sums", wi)))
                eng.matmul(pb[6][0:1, 0:1], onesf_sb[:, :], colx[:, 6:7],
                           start=True, stop=True).then_inc(sT, 1)
                tk.inc(sT)
                tk.put(("d2msum", wi), sT)
                eng.wait_ge(sV, tk.get(("d2mrnd", wi)))
                eng.matmul(pb[7][:, 0:8], onesr_sb[0:1, :], rows6[0:1, 0:8],
                           start=True, stop=True).then_inc(sT, 1)
                tk.inc(sT)
                tk.put(("d2mbc", wi), sT)

            @b.vector
            def _(eng, wi=wi, n_real=n_real):
                eng.wait_ge(sT, tk.get(("d2msum", wi)))
                fence(eng, sV, eng.tensor_scalar(out=rows6[0:1, wi:wi + 1],
                                                 in0=pb[6][0:1, 0:1],
                                                 scalar1=float(1.0 / n_real),
                                                 scalar2=None, op0=ALU.mult))
                fence(eng, sV, eng.tensor_copy(out=d_tb[0:1, 0:1],
                                               in_=rows6[0:1, wi:wi + 1]))
                fence(eng, sV, eng.tensor_copy(out=rows6[0:1, wi:wi + 1],
                                               in_=d_tb[0:1, 0:1]))
                tk.put(("d2mrnd", wi), sV)
                eng.wait_ge(sT, tk.get(("d2mbc", wi)))
                fence(eng, sV, eng.tensor_copy(out=mbc_sb[:, wi:wi + 1],
                                               in_=pb[7][:, wi:wi + 1]))
                fence(eng, sV, eng.tensor_scalar(out=d_scb[:, :], in0=d_sc[:, :],
                                                 scalar1=mbc_sb[:, wi:wi + 1],
                                                 scalar2=None, op0=ALU.subtract))
                eng.reciprocal(out=d_rcpt[:, :], in_=d_sc[:, :])
                for t in range(6):
                    sl = slice(256 * t, 256 * (t + 1))
                    fence(eng, sV, eng.reduce_max(out=colx[:, 8:9], in_=d_scb[:, sl],
                                                  axis=AX.X, apply_absolute_value=True))
                    fence(eng, sV, eng.reciprocal(out=colx[:, 9:10], in_=colx[:, 8:9]))
                    fence(eng, sV, eng.tensor_scalar(out=colx[:, 10:11],
                                                     in0=colx[:, 9:10], scalar1=128.0,
                                                     scalar2=None, op0=ALU.mult))
                    fence(eng, sV, eng.tensor_copy(out=d_tb[:, 0:1], in_=colx[:, 10:11]))
                    fence(eng, sV, eng.tensor_copy(out=colx[:, 10:11], in_=d_tb[:, 0:1]))
                    fence(eng, sV, eng.tensor_scalar(out=d_pp[:, :], in0=d_scb[:, sl],
                                                     scalar1=colx[:, 10:11],
                                                     scalar2=None, op0=ALU.mult))
                    fence(eng, sV, eng.tensor_copy(out=d_qv[:, :], in_=d_pp[:, :]))
                    fence(eng, sV, eng.tensor_copy(out=d_qi[:, :], in_=d_qv[:, :]))
                    fence(eng, sV, eng.tensor_copy(out=d_qv[:, :], in_=d_qi[:, :]))
                    fence(eng, sV, eng.tensor_scalar(out=d_qv[:, :], in0=d_qv[:, :],
                                                     scalar1=-128.0, scalar2=127.0,
                                                     op0=ALU.max, op1=ALU.min))
                    fence(eng, sV, eng.reciprocal(out=colx[:, 11:12],
                                                  in_=colx[:, 10:11]))
                    fence(eng, sV, eng.tensor_scalar(out=d_qv[:, :], in0=d_qv[:, :],
                                                     scalar1=colx[:, 11:12],
                                                     scalar2=None, op0=ALU.mult))
                    fence(eng, sV, eng.tensor_copy(out=d_deqt[:, sl], in_=d_qv[:, :]))
                    fence(eng, sV, eng.tensor_scalar(out=d_deqt[:, sl],
                                                     in0=d_deqt[:, sl],
                                                     scalar1=mbc_sb[:, wi:wi + 1],
                                                     scalar2=None, op0=ALU.add))
                eng.sem_inc(sV, 1)
                tk.inc(sV)
                tk.put(("d2deq", wi), sV)

            @b.sync
            def _(eng, wi=wi):
                eng.wait_ge(sV, tk.get(("d2deq", wi)))
                eng.dma_start(out=deqf[wi][:].rearrange("(a p k) -> p a k", p=128, k=256),
                              in_=d_deqt[:, :].rearrange("p (a k) -> p a k", k=256)
                              ).then_inc(sLd, 16)
                eng.dma_start(out=rcpf[wi][:].rearrange("(a p k) -> p a k", p=128, k=256),
                              in_=d_rcpt[:, :].rearrange("p (a k) -> p a k", k=256)
                              ).then_inc(sLd, 16)
                tk.inc(sLd, 32)
                eng.wait_ge(sLd, tk.val(sLd))
                tk.put(("d2st", wi), sLd)

    with nc.Block() as b:
        @b.sync
        def _(eng):
            for i in range(2 * FC):
                w, t = divmod(i, FC)
                if i >= 2:
                    eng.wait_ge(sV, tk.get(("d3up", i - 2)))
                eng.dma_start(out=d_trL[i % 2][:, 0:32],
                              in_=deqf[w][128 * 32 * t:128 * 32 * (t + 1)]
                              .rearrange("(p k) -> p k", p=128)).then_inc(sLd, 16)
                tk.inc(sLd, 16)
                tk.put(("d3ld", i), sLd)
                eng.wait_ge(sT, tk.get(("d3trA", i)))
                eng.dma_start(out=d_trA[i % 2][:, 0:32],
                              in_=rcpf[w][128 * 32 * t:128 * 32 * (t + 1)]
                              .rearrange("(p k) -> p k", p=128)).then_inc(sLd, 16)
                tk.inc(sLd, 16)
                tk.put(("d3ldr", i), sLd)

        @b.vector
        def _(eng):
            eng.memset(d_trA[0][:, :], 0.0)
            eng.memset(d_trA[1][:, :], 0.0)
            eng.sem_inc(sV, 2)
            tk.inc(sV, 2)
            tk.put(("d3up", -2), sV)
            tk.put(("d3up", -1), sV)
            for i in range(2 * FC):
                w, t = divmod(i, FC)
                eng.wait_ge(sLd, tk.get(("d3ld", i)))
                eng.tensor_copy(out=d_trA[i % 2][:, 0:32], in_=d_trL[i % 2][:, 0:32]
                                ).then_inc(sV, 1)
                tk.inc(sV)
                tk.put(("d3up", i), sV)
                eng.wait_ge(sT, tk.get(("d3trB", i)))
                eng.tensor_copy(out=d_L12d[0:32, FF * w + 128 * t:FF * w + 128 * (t + 1)],
                                in_=pb[i % 2][0:32, 0:128])
                eng.tensor_copy(out=d_L12r[0:32, FF * w + 128 * t:FF * w + 128 * (t + 1)],
                                in_=pb[2 + i % 2][0:32, 0:128]).then_inc(sV, 1)
                tk.inc(sV)
                tk.put(("d3ev", i), sV)

        @b.tensor
        def _(eng):
            for i in range(2 * FC):
                eng.wait_ge(sV, tk.get(("d3up", i)))
                if i >= 2:
                    eng.wait_ge(sV, tk.get(("d3ev", i - 2)))
                eng.transpose(pb[i % 2][:, 0:128], d_trA[i % 2][:, :], idf_sb[:, :]
                              ).then_inc(sT, 1)
                tk.inc(sT)
                tk.put(("d3trA", i), sT)
                eng.wait_ge(sLd, tk.get(("d3ldr", i)))
                eng.transpose(pb[2 + i % 2][:, 0:128], d_trA[i % 2][:, :], idf_sb[:, :]
                              ).then_inc(sT, 1)
                tk.inc(sT)
                tk.put(("d3trB", i), sT)

    with nc.Block() as b:
        @b.sync
        def _(eng):
            for i in range(DC):
                if i >= 2:
                    eng.wait_ge(sT, tk.get(("d35w3", i - 2)))
                eng.dma_start(out=d_s3b[i % 2][:, 0:88],
                              in_=deqf[2][128 * 88 * i:128 * 88 * (i + 1)]
                              .rearrange("(p k) -> p k", p=128)).then_inc(sLd, 16)
                eng.dma_start(out=d_s3f[i % 2][:, 0:88],
                              in_=rcpf[2][128 * 88 * i:128 * 88 * (i + 1)]
                              .rearrange("(p k) -> p k", p=128)).then_inc(sLd, 16)
                eng.dma_start(out=d_sel3b[i % 2][:, :],
                              in_=sel3b[128 * i:128 * (i + 1), :]).then_inc(sLd, 16)
                eng.dma_start(out=d_sel3f[i % 2][:, :],
                              in_=sel3f[128 * i:128 * (i + 1), :]).then_inc(sLd, 16)
                tk.inc(sLd, 64)
                tk.put(("d35ld", i), sLd)

        @b.tensor
        def _(eng):
            nchunk = (2 * FF) // 512
            for ch in range(nchunk):
                if ch >= 1:
                    eng.wait_ge(sC, tk.get(("d35ec", ch - 1)))
                eng.matmul(pb[4][0:4, 0:512], rsb_sb[:, :],
                           d_L12d[0:32, 512 * ch:512 * (ch + 1)], start=True, stop=True)
                eng.matmul(pb[5][0:4, 0:512], rsf_sb[:, :],
                           d_L12r[0:32, 512 * ch:512 * (ch + 1)], start=True, stop=True
                           ).then_inc(sT, 1)
                tk.inc(sT)
                tk.put(("d35sel", ch), sT)
            for i in range(DC):
                eng.wait_ge(sLd, tk.get(("d35ld", i)))
                eng.matmul(pb[6][0:88, 0:256], d_s3b[i % 2][:, 0:88],
                           d_sel3b[i % 2][:, :], start=(i == 0), stop=(i == DC - 1))
                eng.matmul(pb[7][0:88, 0:256], d_s3f[i % 2][:, 0:88],
                           d_sel3f[i % 2][:, :], start=(i == 0), stop=(i == DC - 1)
                           ).then_inc(sT, 1)
                tk.inc(sT)
                tk.put(("d35w3", i), sT)

        @b.scalar
        def _(eng):
            nchunk = (2 * FF) // 512
            for ch in range(nchunk):
                eng.wait_ge(sT, tk.get(("d35sel", ch)))
                eng.copy(out=d_ownb[0:4, 512 * ch:512 * (ch + 1)], in_=pb[4][0:4, 0:512])
                eng.copy(out=d_ownf[0:4, 512 * ch:512 * (ch + 1)], in_=pb[5][0:4, 0:512]
                         ).then_inc(sC, 1)
                tk.inc(sC)
                tk.put(("d35ec", ch), sC)
            tk.put(("d35ev12",), sC)
            eng.wait_ge(sT, tk.get(("d35w3", DC - 1)))
            eng.copy(out=d_ownb[0:88, 2 * FF:2 * FF + 256], in_=pb[6][0:88, 0:256])
            eng.copy(out=d_ownf[0:88, 2 * FF:2 * FF + 256], in_=pb[7][0:88, 0:256]
                     ).then_inc(sC, 1)
            tk.inc(sC)
            tk.put(("d35ev3",), sC)

        @b.gpsimd
        def _(eng):
            eng.wait_ge(sC, tk.get(("d35ev12",)))
            eng.dma_start(out=ownb_d[:, :, :].rearrange("w r f -> r w f"),
                          in_=d_ownb[0:4, 0:2 * FF].rearrange("p (w f) -> p w f", f=FF)
                          ).then_inc(sG, 16)
            eng.dma_start(out=ownf_d[:, :, :].rearrange("w r f -> r w f"),
                          in_=d_ownf[0:4, 0:2 * FF].rearrange("p (w f) -> p w f", f=FF)
                          ).then_inc(sG, 16)
            eng.wait_ge(sC, tk.get(("d35ev3",)))
            eng.dma_start(out=own3b_d[:, :], in_=d_ownb[0:88, 2 * FF:2 * FF + 256]
                          ).then_inc(sG, 16)
            eng.dma_start(out=own3f_d[:, :], in_=d_ownf[0:88, 2 * FF:2 * FF + 256]
                          ).then_inc(sG, 16)
            tk.inc(sG, 64)
            eng.wait_ge(sG, tk.val(sG))
            tk.put(("d35done",), sG)

    for bi, (kind, wi, part) in enumerate(d4b):
        nb = FF if kind == "12" else 22 * 256
        with nc.Block() as b:
            @b.sync
            def _(eng, kind=kind, wi=wi, part=part, bi=bi, nb=nb):
                eng.wait_ge(sG, tk.get(("d35done",)))
                if kind == "12":
                    wts = w1Ts if wi == 0 else w2Ts
                    eng.dma_start(out=d_wsl[:, :], in_=wts[part, :, :]).then_inc(sLd, 16)
                    for half, rr in ((0, 2 * part), (1, 2 * part + 1)):
                        eng.dma_start(out=d_dqb[64 * half:64 * (half + 1), :],
                                      in_=ownb_d[wi, rr, :].unsqueeze(0)
                                      .partition_broadcast(64)).then_inc(sLd, 16)
                        eng.dma_start(out=d_rqb[64 * half:64 * (half + 1), :],
                                      in_=ownf_d[wi, rr, :].unsqueeze(0)
                                      .partition_broadcast(64)).then_inc(sLd, 16)
                    tk.inc(sLd, 80)
                else:
                    for fi in range(22):
                        fc = 22 * part + fi
                        sl = slice(256 * fi, 256 * (fi + 1))
                        eng.dma_start(out=d_wsl[:, sl], in_=w3Ts[fc, :, :]).then_inc(sLd, 16)
                        for half, rr in ((0, 2 * fc), (1, 2 * fc + 1)):
                            eng.dma_start(out=d_dqb[64 * half:64 * (half + 1), sl],
                                          in_=own3b_d[rr, :].unsqueeze(0)
                                          .partition_broadcast(64)).then_inc(sLd, 16)
                            eng.dma_start(out=d_rqb[64 * half:64 * (half + 1), sl],
                                          in_=own3f_d[rr, :].unsqueeze(0)
                                          .partition_broadcast(64)).then_inc(sLd, 16)
                        tk.inc(sLd, 80)
                tk.put(("d4ld", bi), sLd)
                eng.wait_ge(sV, tk.get(("d4wdq", bi)))
                if kind == "12":
                    eng.dma_start(out=wdql[wi][part].rearrange("a p k -> p a k"),
                                  in_=d_wdq[:, :].rearrange("p (a k) -> p a k", k=128)
                                  ).then_inc(sLd, 16)
                else:
                    for s_ in range(2):
                        eng.dma_start(out=wdql[2][s_, 22 * part:22 * (part + 1)]
                                      .rearrange("a p k -> p a k"),
                                      in_=d_wdq[:, 0:nb]
                                      .rearrange("p (a k2) -> p a k2", k2=256)
                                      [:, :, 128 * s_:128 * (s_ + 1)]
                                      ).then_inc(sLd, 16)
                tk.inc(sLd, 32 if kind == "3" else 16)
                eng.wait_ge(sLd, tk.val(sLd))
                tk.put(("d4st", bi), sLd)

            @b.gpsimd
            def _(eng, bi=bi):
                if bi in (1, 3, 5):
                    eng.wait_ge(sLd, tk.get(("d4st", bi)))
                    eng.collective_compute("AllGather", ALU.bypass, replica_groups=RG,
                                           ins=[wdql[bi // 2]], outs=[wdqg[bi // 2]]
                                           ).then_inc(sCC, 1)
                    tk.inc(sCC)
                    tk.put(("wdqg", bi // 2), sCC)

            @b.vector
            def _(eng, bi=bi, nb=nb):
                eng.wait_ge(sLd, tk.get(("d4ld", bi)))
                eng.tensor_tensor(out=d_scl[:, 0:nb], in0=d_wsl[:, 0:nb],
                                  in1=d_rqb[:, 0:nb], op=ALU.mult)
                eng.memset(d_acc[:, 0:nb], 0.0).then_inc(sV, 1)
                tk.inc(sV)
                tk.put(("d4scl", bi), sV)
                for k in range(15):
                    sg = d_sgA if k % 2 == 0 else d_sgB
                    eng.wait_ge(sC, tk.get(("d4sg", bi, k)))
                    eng.scalar_tensor_tensor(out=d_acc[:, 0:nb], in0=sg[:, 0:nb],
                                             scalar=GAP2[k], in1=d_acc[:, 0:nb],
                                             op0=ALU.mult, op1=ALU.add).then_inc(sV, 1)
                    tk.inc(sV)
                    tk.put(("d4st", bi, k), sV)
                eng.tensor_tensor(out=d_wdq[:, 0:nb], in0=d_acc[:, 0:nb],
                                  in1=d_dqb[:, 0:nb], op=ALU.mult).then_inc(sV, 1)
                tk.inc(sV)
                tk.put(("d4wdq", bi), sV)

            @b.scalar
            def _(eng, bi=bi, nb=nb):
                eng.wait_ge(sV, tk.get(("d4scl", bi)))
                for k in range(15):
                    sg = d_sgA if k % 2 == 0 else d_sgB
                    if k >= 2:
                        eng.wait_ge(sV, tk.get(("d4st", bi, k - 2)))
                    eng.activation(out=sg[:, 0:nb], in_=d_scl[:, 0:nb], func=AF.Sign,
                                   bias=taus_sb[:, k:k + 1]).then_inc(sC, 1)
                    tk.inc(sC)
                    tk.put(("d4sg", bi, k), sC)

    for tb in range(4):
        with nc.Block() as b:
            @b.sync
            def _(eng, tb=tb):
                eng.dma_start(out=p_xstg[tb % 2][:, :]
                              .rearrange("p (a k) -> p a k", k=256),
                              in_=xfm[:, 256 * tb:256 * (tb + 1)]
                              .rearrange("(a p) k -> p a k", p=128)).then_inc(sLd, 16)
                tk.inc(sLd, 16)
                tk.put(("p0ld", tb), sLd)

            @b.scalar
            def _(eng, tb=tb):
                eng.wait_ge(sLd, tk.get(("p0ld", tb)))
                eng.copy(out=p_xb[:, :], in_=p_xstg[tb % 2][:, :])
                eng.activation(out=p_x2b[:, :], in_=p_xb[:, :], func=AF.Square
                               ).then_inc(sC, 1)
                tk.inc(sC)
                tk.put(("p0cast", tb), sC)
                eng.wait_ge(sV, tk.get(("p0var", tb)))
                fence(eng, sC, eng.activation(out=rowq[0:1, 256:512],
                                              in_=rowq[0:1, 256:512],
                                              func=AF.Ln, bias=cst_sb[0:1, 0:1]))
                fence(eng, sC, eng.activation(out=rowq[0:1, 256:512],
                                              in_=rowq[0:1, 256:512],
                                              func=AF.Exp, scale=-0.5))
                tk.put(("p0r", tb), sC)
                eng.wait_ge(sV, tk.get(("p0xn", tb)))
                eng.activation(out=p_x2b[:, :],
                               in_=xn_sb[:, :].rearrange("p (a k) -> p a k", k=TLOC)
                               [:, :, 256 * tb:256 * (tb + 1)],
                               func=AF.Square).then_inc(sC, 1)
                tk.inc(sC)
                tk.put(("p0cast2", tb), sC)
                eng.wait_ge(sV, tk.get(("p0var2", tb)))
                fence(eng, sC, eng.activation(out=rowq[0:1, 768:1024],
                                              in_=rowq[0:1, 768:1024],
                                              func=AF.Ln, bias=cst_sb[0:1, 0:1]))
                fence(eng, sC, eng.activation(out=rowq[0:1, 768:1024],
                                              in_=rowq[0:1, 768:1024],
                                              func=AF.Exp, scale=-0.5))
                tk.put(("p0r2", tb), sC)

            @b.tensor
            def _(eng, tb=tb):
                eng.wait_ge(sC, tk.get(("p0cast", tb)))
                for dc in range(DC):
                    eng.matmul(pb[6][0:1, 0:256], ones_sb[:, :],
                               p_xb[:, 256 * dc:256 * (dc + 1)],
                               start=(dc == 0), stop=(dc == DC - 1))
                for dc in range(DC):
                    mm = eng.matmul(pb[6][0:1, 256:512], ones_sb[:, :],
                                    p_x2b[:, 256 * dc:256 * (dc + 1)],
                                    start=(dc == 0), stop=(dc == DC - 1))
                mm.then_inc(sT, 1)
                tk.inc(sT)
                tk.put(("p0mm", tb), sT)
                eng.wait_ge(sV, tk.get(("p0mu", tb)))
                eng.matmul(pb[4][:, 0:256], onesr_sb[0:1, :], rowq[0:1, 0:256],
                           start=True, stop=True)
                eng.wait_ge(sC, tk.get(("p0r", tb)))
                eng.matmul(pb[5][:, 0:256], onesr_sb[0:1, :], rowq[0:1, 256:512],
                           start=True, stop=True).then_inc(sT, 1)
                tk.inc(sT)
                tk.put(("p0bc", tb), sT)
                eng.wait_ge(sV, tk.get(("p0xn", tb)))
                for dc in range(DC):
                    eng.matmul(pb[6][0:1, 0:256], ones_sb[:, :],
                               xn_sb[:, TLOC * dc + 256 * tb:TLOC * dc + 256 * (tb + 1)],
                               start=(dc == 0), stop=(dc == DC - 1))
                eng.wait_ge(sC, tk.get(("p0cast2", tb)))
                for dc in range(DC):
                    mm = eng.matmul(pb[6][0:1, 256:512], ones_sb[:, :],
                                    p_x2b[:, 256 * dc:256 * (dc + 1)],
                                    start=(dc == 0), stop=(dc == DC - 1))
                mm.then_inc(sT, 1)
                tk.inc(sT)
                tk.put(("p0mm2", tb), sT)
                eng.wait_ge(sV, tk.get(("p0mu2", tb)))
                eng.matmul(pb[4][:, 256:512], onesr_sb[0:1, :], rowq[0:1, 512:768],
                           start=True, stop=True)
                eng.wait_ge(sC, tk.get(("p0r2", tb)))
                eng.matmul(pb[5][:, 256:512], onesr_sb[0:1, :], rowq[0:1, 768:1024],
                           start=True, stop=True).then_inc(sT, 1)
                tk.inc(sT)
                tk.put(("p0bc2", tb), sT)

            @b.vector
            def _(eng, tb=tb):
                xn3 = (xn_sb[:, :].rearrange("p (a k) -> p a k", k=TLOC)
                       [:, :, 256 * tb:256 * (tb + 1)])
                h23 = (p_h2[:, :].rearrange("p (a k) -> p a k", k=TLOC)
                       [:, :, 256 * tb:256 * (tb + 1)])
                t13 = p_t1[:, :].rearrange("p (a k) -> p a k", k=256)
                eng.wait_ge(sT, tk.get(("p0mm", tb)))
                fence(eng, sV, eng.tensor_scalar(out=rowq[0:1, 0:256],
                                                 in0=pb[6][0:1, 0:256],
                                                 scalar1=float(1.0 / D), scalar2=None,
                                                 op0=ALU.mult))
                fence(eng, sV, eng.tensor_scalar(out=rowq[0:1, 256:512],
                                                 in0=pb[6][0:1, 256:512],
                                                 scalar1=float(1.0 / D), scalar2=None,
                                                 op0=ALU.mult))
                fence(eng, sV, eng.tensor_tensor(out=rowl[0:1, 0:256],
                                                 in0=rowq[0:1, 0:256],
                                                 in1=rowq[0:1, 0:256], op=ALU.mult))
                fence(eng, sV, eng.tensor_tensor(out=rowq[0:1, 256:512],
                                                 in0=rowq[0:1, 256:512],
                                                 in1=rowl[0:1, 0:256], op=ALU.subtract))
                tk.put(("p0mu", tb), sV)
                tk.put(("p0var", tb), sV)
                eng.wait_ge(sT, tk.get(("p0bc", tb)))
                eng.tensor_tensor(out=t13,
                                  in0=p_xstg[tb % 2][:, :]
                                  .rearrange("p (a k) -> p a k", k=256),
                                  in1=pb[4][:, 0:256].unsqueeze(1)
                                  .broadcast_to([128, DC, 256]),
                                  op=ALU.subtract)
                eng.tensor_tensor(out=xn3, in0=t13,
                                  in1=pb[5][:, 0:256].unsqueeze(1)
                                  .broadcast_to([128, DC, 256]), op=ALU.mult).then_inc(sV, 1)
                tk.inc(sV)
                tk.put(("p0xn", tb), sV)
                eng.wait_ge(sT, tk.get(("p0mm2", tb)))
                fence(eng, sV, eng.tensor_scalar(out=rowq[0:1, 512:768],
                                                 in0=pb[6][0:1, 0:256],
                                                 scalar1=float(1.0 / D), scalar2=None,
                                                 op0=ALU.mult))
                fence(eng, sV, eng.tensor_scalar(out=rowq[0:1, 768:1024],
                                                 in0=pb[6][0:1, 256:512],
                                                 scalar1=float(1.0 / D), scalar2=None,
                                                 op0=ALU.mult))
                fence(eng, sV, eng.tensor_tensor(out=rowl[0:1, 0:256],
                                                 in0=rowq[0:1, 512:768],
                                                 in1=rowq[0:1, 512:768], op=ALU.mult))
                fence(eng, sV, eng.tensor_tensor(out=rowq[0:1, 768:1024],
                                                 in0=rowq[0:1, 768:1024],
                                                 in1=rowl[0:1, 0:256], op=ALU.subtract))
                tk.put(("p0mu2", tb), sV)
                tk.put(("p0var2", tb), sV)
                eng.wait_ge(sT, tk.get(("p0bc2", tb)))
                eng.tensor_tensor(out=t13, in0=xn3,
                                  in1=pb[4][:, 256:512].unsqueeze(1)
                                  .broadcast_to([128, DC, 256]), op=ALU.subtract)
                eng.tensor_tensor(out=t13, in0=t13,
                                  in1=pb[5][:, 256:512].unsqueeze(1)
                                  .broadcast_to([128, DC, 256]), op=ALU.mult)
                eng.tensor_tensor(out=t13, in0=t13,
                                  in1=gam_sb[:, :].unsqueeze(2)
                                  .broadcast_to([128, DC, 256]), op=ALU.mult)
                eng.tensor_tensor(out=h23, in0=t13,
                                  in1=bet_sb[:, :].unsqueeze(2)
                                  .broadcast_to([128, DC, 256]), op=ALU.add).then_inc(sV, 1)
                tk.inc(sV)
                tk.put(("p0h2", tb), sV)

    with nc.Block() as b:
        @b.sync
        def _(eng):
            for i, ec in enumerate(ec_order):
                if i >= 2:
                    eng.wait_ge(sC, tk.get(("p1cast", i - 2)))
                eng.dma_start(out=p_estg[i % 2][:, :]
                              .rearrange("p (a k) -> p a k", k=128),
                              in_=eT[ec, :, :, :].rearrange("a p k -> p a k")
                              ).then_inc(sLd, 16)
                tk.inc(sLd, 16)
                tk.put(("p1ld", i), sLd)

        @b.scalar
        def _(eng):
            for i, ec in enumerate(ec_order):
                eng.wait_ge(sLd, tk.get(("p1ld", i)))
                if i >= 2:
                    eng.wait_ge(sT, tk.get(("p1mm", i - 2)))
                eng.copy(out=p_ebf[i % 2][:, :], in_=p_estg[i % 2][:, :]).then_inc(sC, 1)
                tk.inc(sC)
                tk.put(("p1cast", i), sC)
                eng.wait_ge(sT, tk.get(("p1mm", i)))
                pA, pB = pb[2 * (i % 2)], pb[2 * (i % 2) + 1]
                if ec in (0, 1):
                    eng.copy(out=p_q[:, TLOC * ec:TLOC * ec + 512], in_=pA[:, :])
                    eng.copy(out=p_q[:, TLOC * ec + 512:TLOC * (ec + 1)], in_=pB[:, :]
                             ).then_inc(sC, 1)
                elif ec in (2, 3):
                    eng.copy(out=p_kown[:, TLOC * (ec - 2):TLOC * (ec - 2) + 512],
                             in_=pA[:, :])
                    eng.copy(out=p_kown[:, TLOC * (ec - 2) + 512:TLOC * (ec - 1)],
                             in_=pB[:, :]).then_inc(sC, 1)
                elif ec < 36:
                    if ec >= 5:
                        eng.wait_ge(sV, tk.get(("p1gg", ec - 5)))
                    eng.copy(out=p_lin[:, 0:512], in_=pA[:, :])
                    eng.copy(out=p_lin[:, 512:1024], in_=pB[:, :]).then_inc(sC, 1)
                else:
                    eng.activation(out=p_gel[:, 0:512], in_=pA[:, :], func=AF.Gelu)
                    eng.activation(out=p_gel[:, 512:1024], in_=pB[:, :], func=AF.Gelu
                                   ).then_inc(sC, 1)
                tk.inc(sC)
                tk.put(("p1ev", i), sC)

        @b.tensor
        def _(eng):
            for i, ec in enumerate(ec_order):
                eng.wait_ge(sC, tk.get(("p1cast", i)))
                if i >= 2:
                    eng.wait_ge(sC, tk.get(("p1ev", i - 2)))
                pA, pB = pb[2 * (i % 2)], pb[2 * (i % 2) + 1]
                for dc in range(DC):
                    lhs = p_ebf[i % 2][:, 128 * dc:128 * (dc + 1)]
                    eng.matmul(pA[:, :], lhs, p_h2[:, TLOC * dc:TLOC * dc + 512],
                               start=(dc == 0), stop=(dc == DC - 1))
                    mm = eng.matmul(pB[:, :], lhs, p_h2[:, TLOC * dc + 512:TLOC * (dc + 1)],
                                    start=(dc == 0), stop=(dc == DC - 1))
                mm.then_inc(sT, 1)
                tk.inc(sT)
                tk.put(("p1mm", i), sT)

        @b.vector
        def _(eng):
            for i, ec in enumerate(ec_order):
                if ec < 36:
                    continue
                pre_i = ec - 36
                eng.wait_ge(sC, tk.get(("p1ev", i)))
                if pre_i < 16:
                    dst = p_gst[pre_i % 2][:, :]
                    if pre_i >= 2:
                        eng.wait_ge(sG, tk.get(("p1sp", pre_i - 2)))
                else:
                    dst = p_vfm[:, TLOC * (pre_i - 16):TLOC * (pre_i - 15)]
                eng.tensor_tensor(out=dst, in0=p_lin[:, :], in1=p_gel[:, :],
                                  op=ALU.mult).then_inc(sV, 1)
                tk.inc(sV)
                tk.put(("p1gg", pre_i), sV)

        @b.gpsimd
        def _(eng):
            for pre_i in range(16):
                eng.wait_ge(sV, tk.get(("p1gg", pre_i)))
                eng.dma_start(out=gegsp[pre_i, :, :], in_=p_gst[pre_i % 2][:, :]
                              ).then_inc(sG, 16)
                tk.inc(sG, 16)
                tk.put(("p1sp", pre_i), sG)
            eng.wait_ge(sG, tk.val(sG))
            tk.put(("p1spdone",), sG)

    with nc.Block() as b:
        @b.vector
        def _(eng):
            for i in range(128):
                vc, tc = divmod(i, 8)
                eng.wait_ge(sT, tk.get(("p15tr", i - 2))) if i >= 2 else None
                eng.tensor_copy(out=p_trV[i % 2][:, :],
                                in_=p_vfm[:, TLOC * vc + 128 * tc:TLOC * vc + 128 * (tc + 1)]
                                ).then_inc(sV, 1)
                tk.inc(sV)
                tk.put(("p15up", i), sV)
                eng.wait_ge(sT, tk.get(("p15trd", i)))
                eng.tensor_copy(out=p_vtm[:, 2048 * tc + 128 * vc:2048 * tc + 128 * (vc + 1)],
                                in_=pb[4 + i % 2][:, 0:128]).then_inc(sV, 1)
                tk.inc(sV)
                tk.put(("p15ev", i), sV)

        @b.tensor
        def _(eng):
            for i in range(128):
                eng.wait_ge(sV, tk.get(("p15up", i)))
                if i >= 2:
                    eng.wait_ge(sV, tk.get(("p15ev", i - 2)))
                eng.transpose(pb[4 + i % 2][:, 0:128], p_trV[i % 2][:, :], idf_sb[:, :]
                              ).then_inc(sT, 1)
                tk.inc(sT)
                tk.put(("p15trd", i), sT)
                tk.put(("p15tr", i), sT)

        @b.sync
        def _(eng):
            eng.wait_ge(sV, tk.get(("p15ev", 127)))
            for qkc in range(2):
                eng.dma_start(out=kin[qkc, :, :], in_=p_kown[:, TLOC * qkc:TLOC * (qkc + 1)]
                              ).then_inc(sLd, 16)
            for tc in range(8):
                for vp in range(8):
                    eng.dma_start(out=vin[vp, 128 * tc:128 * (tc + 1), :],
                                  in_=p_vtm[:, 2048 * tc + 256 * vp:2048 * tc + 256 * (vp + 1)]
                                  ).then_inc(sLd, 16)
            tk.inc(sLd, 16 * 66)
            eng.wait_ge(sLd, tk.val(sLd))
            tk.put(("p15kv",), sLd)

        @b.gpsimd
        def _(eng):
            eng.wait_ge(sLd, tk.get(("p15kv",)))
            eng.collective_compute("AllGather", ALU.bypass, replica_groups=RG,
                                   ins=[kin], outs=[kgd]).then_inc(sCC, 1)
            tk.inc(sCC)
            tk.put(("kgath",), sCC)
            eng.collective_compute("AllGather", ALU.bypass, replica_groups=RG,
                                   ins=[vin], outs=[vgd]).then_inc(sCC, 1)
            tk.inc(sCC)
            tk.put(("vgath",), sCC)

    with nc.Block() as b:
        @b.sync
        def _(eng):
            eng.wait_ge(sCC, tk.get(("kgath",)))
            for r in range(NCORE):
                for qkc in range(2):
                    eng.dma_start(out=p_kgs[:, 8192 * qkc + TLOC * r:8192 * qkc + TLOC * (r + 1)],
                                  in_=kgd[r, qkc, :, :]).then_inc(sLd, 16)
            tk.inc(sLd, 256)
            eng.wait_ge(sLd, tk.val(sLd))
            tk.put(("p15kld",), sLd)
            eng.dma_start(out=colx[0:1, 12:13], in_=posm[0:1, :]).then_inc(sLd, 16)
            tk.inc(sLd, 16)
            tk.put(("p15pm",), sLd)

        @b.scalar
        def _(eng):
            eng.wait_ge(sLd, tk.get(("p15pm",)))
            fence(eng, sC, eng.activation(out=colx[0:1, 13:14], in_=colx[0:1, 12:13],
                                          func=AF.Abs))
            fence(eng, sC, eng.activation(out=colx[0:1, 13:14], in_=colx[0:1, 13:14],
                                          func=AF.Exp, scale=-1.0))
            fence(eng, sC, eng.activation(out=colx[0:1, 13:14], in_=colx[0:1, 13:14],
                                          func=AF.Ln, bias=cst_sb[0:1, 3:4]))
            fence(eng, sC, eng.activation(out=colx[0:1, 14:15], in_=colx[0:1, 12:13],
                                          func=AF.Relu))
            tk.put(("p15sp1",), sC)
            eng.activation(out=p_q2b[:, :], in_=p_q[:, :], func=AF.Square).then_inc(sC, 1)
            tk.inc(sC)
            tk.put(("p15q2",), sC)
            eng.wait_ge(sLd, tk.get(("p15kld",)))
            for reg in range(16):
                for qkc in range(2):
                    eng.activation(out=p_k2s[reg % 2][:, 0:512],
                                   in_=p_kgs[:, 8192 * qkc + 512 * reg:8192 * qkc + 512 * (reg + 1)],
                                   func=AF.Square).then_inc(sC, 1)
                    tk.inc(sC)
                    tk.put(("p15k2", reg, qkc), sC)
                    eng.wait_ge(sT, tk.get(("p15k2mmp", reg, qkc)))
            eng.wait_ge(sV, tk.get(("p15t",)))
            fence(eng, sC, eng.activation(out=rowq[0:1, :], in_=rowq[0:1, :], func=AF.Ln))
            fence(eng, sC, eng.activation(out=rowq[0:1, :], in_=rowq[0:1, :], func=AF.Exp,
                                          scale=0.5, bias=cst_sb[0:1, 1:2]))
            tk.put(("p15mb",), sC)

        @b.vector
        def _(eng):
            eng.wait_ge(sC, tk.get(("p15sp1",)))
            fence(eng, sV, eng.tensor_tensor(out=colx[0:1, 15:16], in0=colx[0:1, 13:14],
                                             in1=colx[0:1, 14:15], op=ALU.add))
            tk.put(("p15sp2",), sV)
            fence(eng, sV, eng.memset(rows6[0:1, 4:5], 0.0))
            for reg in range(16):
                eng.wait_ge(sT, tk.get(("p15k2mm", reg)))
                fence(eng, sV, eng.reduce_max(out=rows6[0:1, 5:6], in_=pb[6][0:1, 0:512],
                                              axis=AX.X))
                fence(eng, sV, eng.tensor_tensor(out=rows6[0:1, 4:5], in0=rows6[0:1, 4:5],
                                                 in1=rows6[0:1, 5:6], op=ALU.max))
                tk.put(("p15k2ev", reg), sV)
            eng.wait_ge(sT, tk.get(("p15q2mm",)))
            fence(eng, sV, eng.tensor_copy(out=rowq[0:1, 0:512], in_=pb[7][0:1, 0:512]))
            fence(eng, sV, eng.tensor_copy(out=rowq[0:1, 512:1024], in_=pb[5][0:1, 0:512]))
            fence(eng, sV, eng.tensor_scalar(out=rowq[0:1, :], in0=rowq[0:1, :],
                                             scalar1=rows6[0:1, 4:5], scalar2=None,
                                             op0=ALU.mult))
            tk.put(("p15t",), sV)
            fence(eng, sV, eng.tensor_scalar(out=row64[0:1, :], in0=row64[0:1, :],
                                             scalar1=colx[0:1, 15:16], scalar2=None,
                                             op0=ALU.mult))
            tk.put(("p15spo",), sV)
            eng.wait_ge(sT, tk.get(("p15spobc",)))
            fence(eng, sV, eng.tensor_copy(out=spo_sb[:, :], in_=pb[4][:, 0:64]))
            fence(eng, sV, eng.tensor_scalar(out=spdp_sb[:, :], in0=dp_sb[:, :],
                                             scalar1=spo_sb[:, 2:3], scalar2=None,
                                             op0=ALU.mult))
            tk.put(("p15done",), sV)

        @b.tensor
        def _(eng):
            for reg in range(16):
                for qkc in range(2):
                    eng.wait_ge(sC, tk.get(("p15k2", reg, qkc)))
                    if reg >= 1 and qkc == 0:
                        eng.wait_ge(sV, tk.get(("p15k2ev", reg - 1)))
                    mm = eng.matmul(pb[6][0:1, 0:512], ones_sb[:, :],
                                    p_k2s[reg % 2][:, 0:512],
                                    start=(qkc == 0), stop=(qkc == 1))
                    mm.then_inc(sT, 1)
                    tk.inc(sT)
                    tk.put(("p15k2mmp", reg, qkc), sT)
                tk.put(("p15k2mm", reg), sT)
            eng.wait_ge(sC, tk.get(("p15q2",)))
            for th, bank in ((0, pb[7]), (1, pb[5])):
                for qkc in range(2):
                    mm = eng.matmul(bank[0:1, 0:512], ones_sb[:, :],
                                    p_q2b[:, 1024 * qkc + 512 * th:1024 * qkc + 512 * (th + 1)],
                                    start=(qkc == 0), stop=(qkc == 1))
            mm.then_inc(sT, 1)
            tk.inc(sT)
            tk.put(("p15q2mm",), sT)
            eng.wait_ge(sV, tk.get(("p15spo",)))
            eng.matmul(pb[4][:, 0:64], onesr_sb[0:1, :], row64[0:1, :],
                       start=True, stop=True).then_inc(sT, 1)
            tk.inc(sT)
            tk.put(("p15spobc",), sT)

    with nc.Block() as b:
        @b.sync
        def _(eng):
            eng.dma_start(out=p_msk[:, :].rearrange("p (a k) -> p a k", k=256),
                          in_=masks_i[:, :, :].rearrange("a p k -> p a k")
                          ).then_inc(sLd, 16)
            tk.inc(sLd, 16)
            eng.wait_ge(sLd, tk.val(sLd))
            tk.put(("p2msk",), sLd)

        @b.tensor
        def _(eng):
            eng.wait_ge(sC, tk.get(("p15mb",)))
            for slot in range(NSB):
                eng.matmul(pb[4][:, 0:256], onesr_sb[0:1, :],
                           rowq[0:1, 256 * slot:256 * (slot + 1)],
                           start=True, stop=True).then_inc(sT, 1)
                tk.inc(sT)
                tk.put(("p2mbc", slot), sT)
                eng.wait_ge(sV, tk.get(("p2tb", slot)))
            chunks = [(slot, ci) for slot in range(NSB) for ci in range(NCH[slot])]
            eng.wait_ge(sLd, tk.get(("p2msk",)))
            for gi, (slot, ci) in enumerate(chunks):
                sp_ = ci // 16
                r = (ci % 16) // 2
                jj = ci % 2
                bank = pb[gi % 4]
                if gi >= 4:
                    eng.wait_ge(sV, tk.get(("p2bs", gi - 4)))
                for qkc in range(2):
                    mm = eng.matmul(
                        bank[:, 0:256],
                        p_kgs[:, 8192 * qkc + TLOC * r + 256 * sp_ + 128 * jj:
                              8192 * qkc + TLOC * r + 256 * sp_ + 128 * (jj + 1)],
                        p_q[:, TLOC * qkc + 256 * slot:TLOC * qkc + 256 * (slot + 1)],
                        start=(qkc == 0), stop=(qkc == 1))
                mm.then_inc(sT, 1)
                tk.inc(sT)
                tk.put(("p2s", gi), sT)
                if gi >= 1:
                    pslot, pci = chunks[gi - 1]
                    eng.wait_ge(sC, tk.get(("p2p", gi - 1)))
                    eng.matmul(
                        pb[4 + pslot // 2][0:1, 256 * (pslot % 2):256 * (pslot % 2 + 1)],
                        ones_sb[:, :],
                        p_pc[:, 256 * (SBASE[pslot] + pci):256 * (SBASE[pslot] + pci + 1)],
                        start=(pci == 0), stop=(pci == NCH[pslot] - 1))
            gi = len(chunks) - 1
            pslot, pci = chunks[gi]
            eng.wait_ge(sC, tk.get(("p2p", gi)))
            lm = eng.matmul(
                pb[4 + pslot // 2][0:1, 256 * (pslot % 2):256 * (pslot % 2 + 1)],
                ones_sb[:, :],
                p_pc[:, 256 * (SBASE[pslot] + pci):256 * (SBASE[pslot] + pci + 1)],
                start=(pci == 0), stop=True)
            lm.then_inc(sT, 1)
            tk.inc(sT)
            tk.put(("p2l", NSB - 1), sT)

        @b.vector
        def _(eng):
            for slot in range(NSB):
                eng.wait_ge(sT, tk.get(("p2mbc", slot)))
                eng.tensor_tensor(out=p_tbs[:, 256 * slot:256 * (slot + 1)],
                                  in0=spdp_sb[:, :], in1=pb[4][:, 0:256],
                                  op=ALU.subtract).then_inc(sV, 1)
                tk.inc(sV)
                tk.put(("p2tb", slot), sV)
            chunks = [(slot, ci) for slot in range(NSB) for ci in range(NCH[slot])]
            for gi, (slot, ci) in enumerate(chunks):
                sp_ = ci // 16
                r = (ci % 16) // 2
                jj = ci % 2
                bank = pb[gi % 4]
                eng.wait_ge(sT, tk.get(("p2s", gi)))
                if gi >= 2:
                    eng.wait_ge(sC, tk.get(("p2p", gi - 2)))
                ins = eng.scalar_tensor_tensor(out=p_bS[gi % 2][:, :], in0=bank[:, 0:256],
                                               scalar=1.0 / 16.0,
                                               in1=p_tbs[:, 256 * slot:256 * (slot + 1)],
                                               op0=ALU.mult, op1=ALU.add)
                if sp_ == slot:
                    ins = eng.tensor_tensor(out=p_bS[gi % 2][:, :], in0=p_bS[gi % 2][:, :],
                                            in1=p_msk[:, 256 * (r * 2 + jj):
                                                      256 * (r * 2 + jj + 1)],
                                            op=ALU.add)
                ins.then_inc(sV, 1)
                tk.inc(sV)
                tk.put(("p2bs", gi), sV)

        @b.scalar
        def _(eng):
            chunks = [(slot, ci) for slot in range(NSB) for ci in range(NCH[slot])]
            for gi, (slot, ci) in enumerate(chunks):
                sp_ = ci // 16
                r = (ci % 16) // 2
                jj = ci % 2
                oidx = (slot - sp_) * 16 + r * 2 + jj
                eng.wait_ge(sV, tk.get(("p2bs", gi)))
                eng.activation(out=p_pc[:, 256 * (SBASE[slot] + ci):
                                        256 * (SBASE[slot] + ci + 1)],
                               in_=p_bS[gi % 2][:, :], func=AF.Exp,
                               bias=spo_sb[:, oidx:oidx + 1]).then_inc(sC, 1)
                tk.inc(sC)
                tk.put(("p2p", gi), sC)

    if dbg:
        with nc.Block() as b:
            @b.gpsimd
            def _(eng):
                if "dbg_rowq" in dbg:
                    eng.dma_start(out=dbg["dbg_rowq"][:, :], in_=rowq[0:1, :]).then_inc(sG, 16)
                    tk.inc(sG, 16)
                if "dbg_spo" in dbg:
                    eng.dma_start(out=dbg["dbg_spo"][:, :], in_=spo_sb[:, :]).then_inc(sG, 16)
                    tk.inc(sG, 16)
                if "dbg_spdp" in dbg:
                    eng.dma_start(out=dbg["dbg_spdp"][:, :], in_=spdp_sb[:, :]).then_inc(sG, 16)
                    tk.inc(sG, 16)
                if "dbg_tbs" in dbg:
                    eng.dma_start(out=dbg["dbg_tbs"][:, :], in_=p_tbs[:, :]).then_inc(sG, 16)
                    tk.inc(sG, 16)
                if "dbg_pcc" in dbg:
                    eng.dma_start(out=dbg["dbg_pcc"][:, :], in_=p_pc[:, :]).then_inc(sG, 16)
                    tk.inc(sG, 16)
                if "dbg_b" in dbg:
                    eng.dma_start(out=dbg["dbg_b"][:, :], in_=p_rl[:, :]).then_inc(sG, 16)
                    tk.inc(sG, 16)
                if "dbg_msk" in dbg:
                    eng.dma_start(out=dbg["dbg_msk"][:, :], in_=p_msk[:, :]).then_inc(sG, 16)
                    tk.inc(sG, 16)
                eng.wait_ge(sG, tk.val(sG))

    with nc.Block() as b:
        @b.vector
        def _(eng):
            eng.wait_ge(sT, tk.get(("p2l", NSB - 1)))
            for slot in range(NSB):
                fence(eng, sV, eng.reciprocal(
                    out=rowl[0:1, 256 * slot:256 * (slot + 1)],
                    in_=pb[4 + slot // 2][0:1, 256 * (slot % 2):256 * (slot % 2 + 1)]))
            tk.put(("p25rl",), sV)
            for slot in range(NSB):
                eng.wait_ge(sT, tk.get(("p25bc", slot)))
                eng.tensor_copy(out=p_rl[:, 256 * slot:256 * (slot + 1)],
                                in_=pb[6][:, 0:256]).then_inc(sV, 1)
                tk.inc(sV)
                tk.put(("p25ev", slot), sV)

        @b.tensor
        def _(eng):
            eng.wait_ge(sV, tk.get(("p25rl",)))
            for slot in range(NSB):
                if slot >= 1:
                    eng.wait_ge(sV, tk.get(("p25ev", slot - 1)))
                eng.matmul(pb[6][:, 0:256], onesr_sb[0:1, :],
                           rowl[0:1, 256 * slot:256 * (slot + 1)],
                           start=True, stop=True).then_inc(sT, 1)
                tk.inc(sT)
                tk.put(("p25bc", slot), sT)

    if dbg and "dbg_rl" in dbg:
        with nc.Block() as b:
            @b.gpsimd
            def _(eng):
                eng.wait_ge(sV, tk.get(("p25ev", NSB - 1)))
                eng.dma_start(out=dbg["dbg_rl"][:, :], in_=p_rl[:, :]).then_inc(sG, 16)
                tk.inc(sG, 16)
                eng.wait_ge(sG, tk.val(sG))

    for vp in range(8):
        with nc.Block() as b:
            @b.sync
            def _(eng, vp=vp):
                if vp == 0:
                    eng.wait_ge(sCC, tk.get(("vgath",)))
                if vp >= 2:
                    eng.wait_ge(sT, tk.get(("p3mm", vp - 2)))
                for cg in range(64):
                    sp_ = cg // 16
                    r = (cg % 16) // 2
                    jj = cg % 2
                    eng.dma_start(
                        out=p_vpan[vp % 2][:, 256 * cg:256 * (cg + 1)],
                        in_=vgd[r, vp, 256 * sp_ + 128 * jj:256 * sp_ + 128 * (jj + 1), :]
                        ).then_inc(sLd, 16)
                tk.inc(sLd, 64 * 16)
                eng.wait_ge(sLd, tk.val(sLd))
                tk.put(("p3ld", vp), sLd)

            @b.tensor
            def _(eng, vp=vp):
                eng.wait_ge(sLd, tk.get(("p3ld", vp)))
                if vp == 0:
                    eng.wait_ge(sV, tk.get(("p25ev", NSB - 1)))
                if vp >= 1:
                    eng.wait_ge(sV, tk.get(("p3ev", vp - 1)))
                mm = None
                for cg in range(64):
                    sp_ = cg // 16
                    for vc in range(2):
                        lhs = p_vpan[vp % 2][:, 256 * cg + 128 * vc:256 * cg + 128 * (vc + 1)]
                        for slot in range(sp_, NSB):
                            mm = eng.matmul(
                                pb[slot * 2 + vc][:, 0:256], lhs,
                                p_pc[:, 256 * (SBASE[slot] + cg):256 * (SBASE[slot] + cg + 1)],
                                start=(cg == 0), stop=(cg == NCH[slot] - 1))
                mm.then_inc(sT, 1)
                tk.inc(sT)
                tk.put(("p3mm", vp), sT)

            @b.vector
            def _(eng, vp=vp):
                eng.wait_ge(sT, tk.get(("p3mm", vp)))
                if vp >= 1:
                    eng.wait_ge(sC, tk.get(("p3st", vp - 1)))
                ins = None
                for slot in range(NSB):
                    for vc in range(2):
                        ins = eng.tensor_tensor(out=p_ast[:, 256 * (slot * 2 + vc):
                                                          256 * (slot * 2 + vc + 1)],
                                                in0=pb[slot * 2 + vc][:, 0:256],
                                                in1=p_rl[:, 256 * slot:256 * (slot + 1)],
                                                op=ALU.mult)
                ins.then_inc(sV, 1)
                tk.inc(sV)
                tk.put(("p3ev", vp), sV)

            @b.scalar
            def _(eng, vp=vp):
                eng.wait_ge(sV, tk.get(("p3ev", vp)))
                for slot in range(NSB):
                    for vc in range(2):
                        eng.dma_start(out=attnb[vp * 2 + vc, slot, :, :],
                                      in_=p_ast[:, 256 * (slot * 2 + vc):
                                                256 * (slot * 2 + vc + 1)]).then_inc(sC, 16)
                tk.inc(sC, 8 * 16)
                eng.wait_ge(sC, tk.val(sC))
                tk.put(("p3st", vp), sC)

    with nc.Block() as b:
        @b.sync
        def _(eng):
            eng.wait_ge(sG, tk.get(("p1spdone",)))
            for i in range(16):
                eng.dma_start(out=p_glo[:, TLOC * i:TLOC * (i + 1)], in_=gegsp[i, :, :]
                              ).then_inc(sLd, 16)
            eng.wait_ge(sC, tk.get(("p3st", 7)))
            for vc in range(16):
                for slot in range(NSB):
                    eng.dma_start(out=p_att[:, TLOC * vc + 256 * slot:TLOC * vc + 256 * (slot + 1)],
                                  in_=attnb[vc, slot, :, :]).then_inc(sLd, 16)
            tk.inc(sLd, 80 * 16)
            eng.wait_ge(sLd, tk.val(sLd))
            tk.put(("p4act",), sLd)
            for dc in range(DC):
                if dc >= 2:
                    eng.wait_ge(sC, tk.get(("p4cast", dc - 2)))
                eng.dma_start(out=p_pstg[dc % 2][:, :]
                              .rearrange("p (a k) -> p a k", k=128),
                              in_=pT[dc, :, :, :].rearrange("a p k -> p a k")
                              ).then_inc(sLd, 16)
                tk.inc(sLd, 16)
                tk.put(("p4ld", dc), sLd)

        @b.scalar
        def _(eng):
            for dc in range(DC):
                eng.wait_ge(sLd, tk.get(("p4ld", dc)))
                if dc >= 2:
                    eng.wait_ge(sT, tk.get(("p4mm", dc - 2)))
                eng.copy(out=p_pbf[dc % 2][:, :], in_=p_pstg[dc % 2][:, :]).then_inc(sC, 1)
                tk.inc(sC)
                tk.put(("p4cast", dc), sC)
                eng.wait_ge(sT, tk.get(("p4mm", dc)))
                pA, pB = pb[2 * (dc % 2)], pb[2 * (dc % 2) + 1]
                eng.copy(out=p_prj[:, TLOC * dc:TLOC * dc + 512], in_=pA[:, :])
                eng.copy(out=p_prj[:, TLOC * dc + 512:TLOC * (dc + 1)], in_=pB[:, :]
                         ).then_inc(sC, 1)
                tk.inc(sC)
                tk.put(("p4ev", dc), sC)

        @b.tensor
        def _(eng):
            eng.wait_ge(sLd, tk.get(("p4act",)))
            for dc in range(DC):
                eng.wait_ge(sC, tk.get(("p4cast", dc)))
                if dc >= 2:
                    eng.wait_ge(sC, tk.get(("p4ev", dc - 2)))
                pA, pB = pb[2 * (dc % 2)], pb[2 * (dc % 2) + 1]
                for ec in range(32):
                    src = p_glo if ec < 16 else p_att
                    e2 = ec % 16
                    lhs = p_pbf[dc % 2][:, 128 * ec:128 * (ec + 1)]
                    eng.matmul(pA[:, :], lhs, src[:, TLOC * e2:TLOC * e2 + 512],
                               start=(ec == 0), stop=(ec == 31))
                    mm = eng.matmul(pB[:, :], lhs, src[:, TLOC * e2 + 512:TLOC * (e2 + 1)],
                                    start=(ec == 0), stop=(ec == 31))
                mm.then_inc(sT, 1)
                tk.inc(sT)
                tk.put(("p4mm", dc), sT)

    with nc.Block() as b:
        @b.sync
        def _(eng):
            eng.wait_ge(sCC, tk.get(("wdqg", 0)))
            for fc in range(FC):
                w = 0
                if fc >= 2:
                    eng.wait_ge(sT, tk.get(("p5mm", fc - 2, w)))
                for dc in range(DC):
                    r, s_ = dc // 2, dc % 2
                    eng.dma_start(out=p_wst[w][fc % 2][:, 128 * dc:128 * (dc + 1)],
                                  in_=wdqg[w][r, s_, fc, :, :]).then_inc(sLd, 16)
                tk.inc(sLd, 16 * DC)
                tk.put(("p5ld", fc, w), sLd)



        @b.tensor
        def _(eng):
            for fc in range(FC):
                for w in range(2):
                    if w == 0:
                        eng.wait_ge(sLd, tk.get(("p5ld", fc, w)))
                    else:
                        eng.wait_ge(sC, tk.get(("p5ldg", fc, w)))
                    if fc >= 2:
                        eng.wait_ge(sV, tk.get(("p5h", fc - 2)))
                    pA = pb[4 * (fc % 2) + 2 * w]
                    pB = pb[4 * (fc % 2) + 2 * w + 1]
                    for dc in range(DC):
                        lhs = p_wst[w][fc % 2][:, 128 * dc:128 * (dc + 1)]
                        eng.matmul(pA[:, :], lhs, xn_sb[:, TLOC * dc:TLOC * dc + 512],
                                   start=(dc == 0), stop=(dc == DC - 1))
                        mm = eng.matmul(pB[:, :], lhs,
                                        xn_sb[:, TLOC * dc + 512:TLOC * (dc + 1)],
                                        start=(dc == 0), stop=(dc == DC - 1))
                    eng.matmul(pA[:, :], zeros_sb[:, 0:128], zeros_sb[:, :],
                               start=False, stop=True)
                    mm = eng.matmul(pB[:, :], zeros_sb[:, 0:128], zeros_sb[:, :],
                                    start=False, stop=True)
                    mm.then_inc(sT, 1)
                    tk.inc(sT)
                    tk.put(("p5mm", fc, w), sT)

        @b.scalar
        def _(eng):
            def ld1(fc):
                if fc >= FC:
                    return
                if fc >= 2:
                    eng.wait_ge(sT, tk.get(("p5mm", fc - 2, 1)))
                for dc in range(DC):
                    r, s_ = dc // 2, dc % 2
                    eng.dma_start(out=p_wst[1][fc % 2][:, 128 * dc:128 * (dc + 1)],
                                  in_=wdqg[1][r, s_, fc, :, :]).then_inc(sC, 16)
                tk.inc(sC, 16 * DC)
                tk.put(("p5ldg", fc, 1), sC)

            eng.wait_ge(sCC, tk.get(("wdqg", 1)))
            ld1(0)
            ld1(1)
            for fc in range(FC):
                eng.wait_ge(sT, tk.get(("p5mm", fc, 0)))
                if fc >= 2:
                    eng.wait_ge(sV, tk.get(("p5h", fc - 2)))
                eng.activation(out=p_sil[fc % 2][:, 0:512], in_=pb[4 * (fc % 2)][:, :],
                               func=AF.Silu)
                eng.activation(out=p_sil[fc % 2][:, 512:1024], in_=pb[4 * (fc % 2) + 1][:, :],
                               func=AF.Silu).then_inc(sC, 1)
                tk.inc(sC)
                tk.put(("p5sil", fc), sC)
                ld1(fc + 2)

        @b.vector
        def _(eng):
            for fc in range(FC):
                eng.wait_ge(sC, tk.get(("p5sil", fc)))
                eng.wait_ge(sT, tk.get(("p5mm", fc, 1)))
                eng.tensor_tensor(out=p_h[:, TLOC * fc:TLOC * fc + 512],
                                  in0=p_sil[fc % 2][:, 0:512], in1=pb[4 * (fc % 2) + 2][:, :],
                                  op=ALU.mult)
                eng.tensor_tensor(out=p_h[:, TLOC * fc + 512:TLOC * (fc + 1)],
                                  in0=p_sil[fc % 2][:, 512:1024],
                                  in1=pb[4 * (fc % 2) + 3][:, :],
                                  op=ALU.mult).then_inc(sV, 1)
                tk.inc(sV)
                tk.put(("p5h", fc), sV)

    with nc.Block() as b:
        @b.sync
        def _(eng):
            for dc in range(DC):
                if dc >= 1:
                    eng.wait_ge(sOut, tk.get(("p6st", dc - 1)))
                eng.dma_start(out=p_xst2[:, :], in_=xfm[128 * dc:128 * (dc + 1), :]
                              ).then_inc(sLd, 16)
                tk.inc(sLd, 16)
                tk.put(("p6x", dc), sLd)

        @b.gpsimd
        def _(eng):
            eng.wait_ge(sCC, tk.get(("wdqg", 2)))
            eng.sem_inc(sG, 1)
            tk.inc(sG, 1)
            tk.put(("p6cc",), sG)



        @b.tensor
        def _(eng):
            for dc in range(DC):
                pA, pB = pb[2 * (dc % 2)], pb[2 * (dc % 2) + 1]
                mm = None
                for hh in range(2):
                    eng.wait_ge(sC, tk.get(("p6ld", dc, hh)))
                    if dc >= 2 and hh == 0:
                        eng.wait_ge(sV, tk.get(("p6ev", dc - 2)))
                    for fi in range(22):
                        fc = 22 * hh + fi
                        lhs = p_w3st[hh][:, 128 * fi:128 * (fi + 1)]
                        eng.matmul(pA[:, :], lhs, p_h[:, TLOC * fc:TLOC * fc + 512],
                                   start=(fc == 0), stop=(fc == FC - 1))
                        mm = eng.matmul(pB[:, :], lhs,
                                        p_h[:, TLOC * fc + 512:TLOC * (fc + 1)],
                                        start=(fc == 0), stop=(fc == FC - 1))
                    mm.then_inc(sT, 1)
                    tk.inc(sT)
                    tk.put(("p6mmh", dc * 2 + hh), sT)
                tk.put(("p6mm", dc), sT)

        @b.vector
        def _(eng):
            for dc in range(DC):
                pA, pB = pb[2 * (dc % 2)], pb[2 * (dc % 2) + 1]
                eng.wait_ge(sT, tk.get(("p6mm", dc)))
                eng.wait_ge(sLd, tk.get(("p6x", dc)))
                if dc >= 1:
                    eng.wait_ge(sG, tk.get(("p6st", dc - 1)))
                ins = None
                for th, bank in ((0, pA), (1, pB)):
                    sl = slice(512 * th, 512 * (th + 1))
                    eng.tensor_tensor(out=p_xst2[:, sl], in0=bank[:, :],
                                      in1=p_xst2[:, sl], op=ALU.add)
                    eng.tensor_tensor(out=p_xst2[:, sl], in0=p_xst2[:, sl],
                                      in1=xn_sb[:, TLOC * dc + sl.start:TLOC * dc + sl.stop],
                                      op=ALU.add)
                    ins = eng.tensor_tensor(out=p_xst2[:, sl], in0=p_xst2[:, sl],
                                            in1=p_prj[:, TLOC * dc + sl.start:TLOC * dc + sl.stop],
                                            op=ALU.add)
                ins.then_inc(sV, 1)
                tk.inc(sV)
                tk.put(("p6ev", dc), sV)

        @b.scalar
        def _(eng):
            def ldw3(dc, hh):
                if dc >= DC:
                    return
                if dc * 2 + hh >= 2:
                    eng.wait_ge(sT, tk.get(("p6mmh", dc * 2 + hh - 2)))
                for fi in range(22):
                    fc = 22 * hh + fi
                    eng.dma_start(out=p_w3st[hh][:, 128 * fi:128 * (fi + 1)],
                                  in_=wdqg[2][dc // 2, dc % 2, fc, :, :]).then_inc(sC, 16)
                tk.inc(sC, 16 * 22)
                tk.put(("p6ld", dc, hh), sC)

            eng.wait_ge(sG, tk.get(("p6cc",)))
            ldw3(0, 0)
            ldw3(0, 1)
            for dc in range(DC):
                ldw3(dc + 1, 0)
                ldw3(dc + 1, 1)
                eng.wait_ge(sV, tk.get(("p6ev", dc)))
                eng.dma_start(out=out_c[128 * dc:128 * (dc + 1), :], in_=p_xst2[:, :]
                              ).then_inc(sOut, 16)
                tk.inc(sOut, 16)
                tk.put(("p6st", dc), sOut)
            eng.wait_ge(sOut, tk.val(sOut))

    if (tap_outs or tap_xn is not None or tap_h2 is not None or tap_pc is not None
            or tap_prj is not None or tap_h is not None):
        with nc.Block() as b:
            @b.gpsimd
            def _(eng):
                for t, dst in tap_outs.items():
                    src = tap_srcs[t]
                    eng.dma_start(out=dst.flatten().rearrange("(p k) -> p k", p=1),
                                  in_=src.flatten().rearrange("(p k) -> p k", p=1)
                                  ).then_inc(sG, 16)
                    tk.inc(sG, 16)
                if tap_xn is not None:
                    eng.dma_start(out=tap_xn[:, :], in_=xn_sb[:, :]).then_inc(sG, 16)
                    tk.inc(sG, 16)
                if tap_h2 is not None:
                    eng.dma_start(out=tap_h2[:, :], in_=p_h2[:, :]).then_inc(sG, 16)
                    tk.inc(sG, 16)
                if tap_pc is not None:
                    eng.dma_start(out=tap_pc[:, :], in_=p_pc[:, :]).then_inc(sG, 16)
                    tk.inc(sG, 16)
                if tap_prj is not None:
                    eng.dma_start(out=tap_prj[:, :], in_=p_prj[:, :]).then_inc(sG, 16)
                    tk.inc(sG, 16)
                if tap_h is not None:
                    eng.dma_start(out=tap_h[:, :], in_=p_h[:, :]).then_inc(sG, 16)
                    tk.inc(sG, 16)
                eng.wait_ge(sG, tk.val(sG))

    for c in reversed(pb_ctxs + sb_ctxs + sem_ctxs):
        c.__exit__(None, None, None)
    return nc, tk.cur


_NC_CACHE = {}


def _get_nc():
    key = tuple(sorted(_TAPS))
    if key not in _NC_CACHE:
        _, toks = _build(None)
        nc, _ = _build(toks)
        _NC_CACHE[key] = nc
    return _NC_CACHE[key]


def kernel(**inputs):
    in_maps = _prep_inputs(inputs)
    nc = _get_nc()
    res = run_bass_kernel_spmd(nc, in_maps, core_ids=list(range(NCORE)))
    out = np.empty((S, D), np.float32)
    for c in range(NCORE):
        out[c::8, :] = res.results[c]["out_c"].T
    return out.reshape(1, S, D)
